# version 08
# baseline (speedup 1.0000x reference)
import numpy as np
import ml_dtypes
from contextlib import ExitStack

import concourse.bass as bass
import concourse.mybir as mybir
from concourse.alu_op_type import AluOpType as OP
from concourse.bass_utils import run_bass_kernel_spmd

F32 = mybir.dt.float32
BF16 = mybir.dt.bfloat16
AF = mybir.ActivationFunctionType
AX = mybir.AxisListType.X

NCORES = 8
SEQ = 2048
E = 2048
NF = 8192
TPC = 4096
SB = 512
NSB = TPC // SB
NT1 = NF // 128
HC = E // 128
EPS = 1e-5
DEBUG_SB_BARRIER = False
DEBUG_LN = False

ENGS = ("tensor", "vector", "scalar", "sync", "gpsimd")


class _Sem:
    __slots__ = ("h", "n")

    def __init__(self, h):
        self.h = h
        self.n = 0

    def bump(self, k=1):
        self.n += k
        return self.n


class _B:

    def __init__(self, nc):
        self.nc = nc
        self.q = {e: [] for e in ENGS}

    def put(self, eng, fn):
        self.q[eng].append(fn)

    def wait(self, eng, sem, val):
        if val > 0:
            self.q[eng].append(lambda en, s=sem.h, v=val: en.wait_ge(s, v))

    def barrier(self, sem):
        for e in ENGS:
            self.q[e].append(lambda en, s=sem.h: en.nop().then_inc(s, 1))
        sem.n += len(ENGS)
        for e in ENGS:
            self.wait(e, sem, sem.n)

    def flush(self, block):
        for ename in ENGS:
            lst = self.q[ename]

            def run(eng, lst=lst):
                for fn in lst:
                    fn(eng)

            getattr(block, ename)(run)
        self.q = {e: [] for e in ENGS}


def _build(stages=("ffn1", "attn0", "attn1", "ffn3"), debug_out=(), nsb=NSB,
           ffn_phases=("z", "m1", "m2", "t")):
    nc = bass.Bass("TRN2", target_bir_lowering=False)

    def dram(name, shape, dtype):
        kd = "ExternalOutput" if name in debug_out else "Internal"
        return nc.dram_tensor(name, shape, dtype, kind=kd).ap()

    x = nc.dram_tensor("x", [TPC, E], F32, kind="ExternalInput").ap()
    wq = nc.dram_tensor("wq", [E, E], F32, kind="ExternalInput").ap()
    wk = nc.dram_tensor("wk", [E, E], F32, kind="ExternalInput").ap()
    wv = nc.dram_tensor("wv", [E, E], F32, kind="ExternalInput").ap()
    wo = nc.dram_tensor("wo", [E, E], F32, kind="ExternalInput").ap()
    w0 = nc.dram_tensor("w0", [NF, E], F32, kind="ExternalInput").ap()
    w1 = nc.dram_tensor("w1", [E, NF], F32, kind="ExternalInput").ap()
    b0r = nc.dram_tensor("b0r", [NT1, 128], F32, kind="ExternalInput").ap()
    b1r = nc.dram_tensor("b1r", [HC, 128], F32, kind="ExternalInput").ap()
    idf = nc.dram_tensor("idf", [128, 128], F32, kind="ExternalInput").ap()
    idb = nc.dram_tensor("idb", [128, 128], BF16, kind="ExternalInput").ap()

    y = nc.dram_tensor("y", [TPC, E], F32, kind="ExternalOutput").ap()

    w0t = dram("w0t", [NT1, HC, 128, 128], BF16)
    w1t = dram("w1t", [HC, NT1, 128, 128], BF16)
    h1 = dram("h1", [TPC, E], BF16)
    h2 = dram("h2", [TPC, E], BF16)
    ytd = dram("ytd", [HC, 128, E], BF16)
    qd = dram("qd", [SEQ, E], BF16)
    vtd = dram("vtd", [E, SEQ], BF16)
    atd = dram("atd", [HC, 128, E], BF16)
    otd = dram("otd", [E, SEQ], BF16)
    dbg = dram("dbg", [32, 128, 8], F32)

    bld = _B(nc)
    sems = {}

    with ExitStack() as ctx:
        en = ctx.enter_context

        def S(name):
            if name not in sems:
                sems[name] = _Sem(en(nc.semaphore(name)))
            return sems[name]

        idb_sb = en(nc.sbuf_tensor("idb_sb", [128, 128], BF16))
        idf_sb = en(nc.sbuf_tensor("idf_sb", [128, 128], F32))
        b0sb = en(nc.sbuf_tensor("b0sb", [128, NT1], F32))
        hb1sb = en(nc.sbuf_tensor("hb1sb", [128, HC], F32))

        def psum_guard():
            for nm in ("gel", "ep", "pev", "exp", "gev", "rv"):
                if nm in sems and sems[nm].n > 0:
                    bld.wait("tensor", sems[nm], sems[nm].n)

        s_stg = S("stg")
        s_rw = S("rw")
        s_rv = S("rv")
        ring_use = [0]

        with ExitStack() as pctx:
            pen = pctx.enter_context
            b0ld = pen(nc.sbuf_tensor("b0ld", [NT1, 128], F32))
            b1ld = pen(nc.sbuf_tensor("b1ld", [HC, 128], F32))
            pb0 = pen(nc.psum_tensor("pb0", [128, NT1], F32))
            pb1 = pen(nc.psum_tensor("pb1", [128, HC], F32))
            s_ini = S("ini")
            s_pp = S("pp")
            bld.put("scalar", lambda e: e.dma_start(idb_sb[:], idb[:, :]).then_inc(s_ini.h, 16))
            bld.put("scalar", lambda e: e.dma_start(idf_sb[:], idf[:, :]).then_inc(s_ini.h, 16))
            bld.put("scalar", lambda e: e.dma_start(b0ld[:], b0r[:, :]).then_inc(s_ini.h, 16))
            bld.put("scalar", lambda e: e.dma_start(b1ld[:], b1r[:, :]).then_inc(s_ini.h, 16))
            s_ini.n = 64
            bld.wait("tensor", s_ini, 64)
            bld.put("tensor", lambda e: e.transpose(pb0[:], b0ld[:], idf_sb[0:NT1, 0:NT1]))
            bld.put("tensor", lambda e: e.transpose(pb1[:], b1ld[:], idf_sb[0:HC, 0:HC]))
            bld.put("tensor", lambda e: e.drain().then_inc(s_pp.h, 1))
            s_pp.n = 1
            bld.wait("vector", s_pp, 1)
            bld.put("vector", lambda e: e.tensor_copy(b0sb[:], pb0[:]))
            bld.put("vector", lambda e: e.tensor_scalar(hb1sb[:], pb1[:], 0.5, None, op0=OP.mult))
            bld.barrier(s_stg)
            with nc.Block() as blk:
                bld.flush(blk)

        pass1_preproc_done = [False]

        def gen_ffn(pass1, ffn):
            src = x if pass1 else h2
            dst = h1 if pass1 else y
            s_xa = S("xa1" if pass1 else "xa3")
            s_xr = S("xr1" if pass1 else "xr3")
            s_lnv = S("lnv")
            s_sq = S("sq")
            s_zap = S("zap")
            s_m1 = S("m1")
            s_gel = S("gel")
            s_m2q = S("m2q")
            s_ep = S("ep")
            s_w0ld = S("w0ld")
            s_w1ld = S("w1ld")
            s_dwr = S("h1wr" if pass1 else "ywr")
            gen = pass1 and not pass1_preproc_done[0]
            if gen:
                s_w0sl = S("w0sl")
                s_w1sl = S("w1sl")
                s_w0wr = S("w0wr")
                s_w1wr = S("w1wr")
            if not pass1 and "h2wr" in sems:
                bld.wait("sync", sems["h2wr"], sems["h2wr"].n)

            psum_guard()
            m1_base = s_m1.n
            m2q_base = s_m2q.n
            gel_base = s_gel.n
            ep_base = s_ep.n
            z_ruse, t_ruse, tt_rv = [], [], []
            pp_w0_ruse, pp_w1_ruse = [], []

            def emit_w0_slab(j):
                if j >= 2:
                    bld.wait("gpsimd", s_rw, pp_w0_ruse[j - 2] + 1)
                bld.put("gpsimd", lambda e, j=j: e.dma_start(
                    ffn.w0sl[:, j % 2, :], w0[j * 128:(j + 1) * 128, :]).then_inc(s_w0sl.h, 16))
                s_w0sl.bump(16)
                u = ring_use[0]
                pp_w0_ruse.append(u)
                bld.wait("tensor", s_w0sl, 16 * (j + 1))
                bld.wait("tensor", s_rv, u)
                for c in range(HC):
                    last = c == HC - 1

                    def tr(e, j=j, c=c, u=u, last=last):
                        i = e.transpose(ffn.ring[:, 0, c, :],
                                        ffn.w0sl[:, j % 2, c * 128:(c + 1) * 128], idb_sb[:])
                    bld.put("tensor", tr)
                bld.put("tensor", lambda e: e.drain().then_inc(s_rw.h, 1))
                s_rw.bump(1)
                ring_use[0] += 1
                bld.wait("vector", s_rw, u + 1)
                if j >= 1:
                    bld.wait("vector", s_w0wr, 16 * j)
                bld.put("vector", lambda e, j=j, u=u: e.tensor_copy(
                    ffn.pst[:], ffn.ring[:, 0, :, :]))
                bld.put("vector", lambda e: e.drain().then_inc(s_rv.h, 1))
                s_rv.bump(1)
                bld.wait("scalar", s_rv, u + 1)
                bld.put("scalar", lambda e, j=j: e.dma_start(
                    w0t[j].rearrange("c h n -> h c n"), ffn.pst[:]).then_inc(s_w0wr.h, 16))
                s_w0wr.bump(16)

            def emit_w1_slab(h):
                pass

            def emit_w1_half(h, half):
                gq_prev = h * 4 + 2 * half - 1
                if gq_prev >= 0:
                    bld.wait("gpsimd", s_rw, pp_w1_ruse[gq_prev] + 1)
                bld.put("gpsimd", lambda e, h=h, half=half: e.dma_start(
                    ffn.w1sl[:, :],
                    w1[h * 128:(h + 1) * 128, half * 4096:(half + 1) * 4096]
                ).then_inc(s_w1sl.h, 16))
                s_w1sl.bump(16)

            def emit_w1_quarters(h):
                for q in range(4):
                    if q % 2 == 0:
                        emit_w1_half(h, q // 2)
                    u = ring_use[0]
                    pp_w1_ruse.append(u)
                    bld.wait("tensor", s_w1sl, 16 * (2 * h + q // 2 + 1))
                    bld.wait("tensor", s_rv, u)
                    for cl in range(16):
                        c = q * 16 + cl
                        coff = c % 32
                        last = cl == 15

                        def tr(e, coff=coff, cl=cl, u=u, last=last):
                            i = e.transpose(ffn.ring[:, 0, cl, :],
                                            ffn.w1sl[:, coff * 128:(coff + 1) * 128], idb_sb[:])
                        bld.put("tensor", tr)
                    bld.put("tensor", lambda e: e.drain().then_inc(s_rw.h, 1))
                    s_rw.bump(1)
                    ring_use[0] += 1
                    bld.wait("vector", s_rw, u + 1)
                    gq = h * 4 + q
                    if gq >= 1:
                        bld.wait("vector", s_w1wr, 16 * gq)
                    bld.put("vector", lambda e, q=q, u=u: e.tensor_copy(
                        ffn.pst[:], ffn.ring[:, 0, :, :]))
                    bld.put("vector", lambda e: e.drain().then_inc(s_rv.h, 1))
                    s_rv.bump(1)
                    bld.wait("scalar", s_rv, u + 1)
                    bld.put("scalar", lambda e, h=h, q=q: e.dma_start(
                        w1t[h, q * 16:(q + 1) * 16].rearrange("c n h2 -> n c h2"),
                        ffn.pst[:]).then_inc(s_w1wr.h, 16))
                    s_w1wr.bump(16)

            zevs = {}

            def emit_z_ln(t):
                row = t * 128
                if t >= 2:
                    bld.wait("sync", s_zap, s_zap.n - 1)
                bld.put("sync", lambda e, row=row, t=t: e.dma_start(
                    ffn.xa[:, t % 2, :], src[row:row + 128, :]).then_inc(s_xa.h, 16))
                s_xa.bump(16)
                bld.wait("vector", s_xa, s_xa.n)
                for k in range(4):
                    bld.put("vector", lambda e, t=t, k=k: e.bn_stats(
                        ffn.lnb[:, t % 2, 6 * k:6 * (k + 1)],
                        ffn.xa[:, t % 2, 512 * k:512 * (k + 1)]))
                bld.put("vector", lambda e: e.drain())
                bld.put("vector", lambda e, t=t: e.bn_aggr(
                    ffn.lnb[:, t % 2, 24:26], ffn.lnb[:, t % 2, 0:24]))
                bld.put("vector", lambda e: e.drain())
                bld.put("vector", lambda e, t=t: e.tensor_scalar(
                    ffn.lnb[:, t % 2, 26:27], ffn.lnb[:, t % 2, 25:26], EPS, None, op0=OP.add
                ))
                bld.put("vector", lambda e: e.drain().then_inc(s_lnv.h, 1))
                s_lnv.bump(1)
                bld.wait("scalar", s_lnv, s_lnv.n)
                bld.put("scalar", lambda e, t=t: e.activation(
                    ffn.lnb[:, t % 2, 27:28], ffn.lnb[:, t % 2, 26:27], AF.Sqrt))
                bld.put("scalar", lambda e: e.drain().then_inc(s_sq.h, 1))
                s_sq.bump(1)
                bld.wait("vector", s_sq, s_sq.n)
                bld.put("vector", lambda e, t=t: e.reciprocal(
                    ffn.lnb[:, t % 2, 28:29], ffn.lnb[:, t % 2, 27:28]))
                bld.put("vector", lambda e: e.drain())
                if t >= 2:
                    bld.wait("vector", s_rw, z_ruse[t - 2] + 1)
                bld.put("vector", lambda e, t=t: e.tensor_scalar(
                    ffn.zst[:, t % 2, :], ffn.xa[:, t % 2, :],
                    ffn.lnb[:, t % 2, 24:25], ffn.lnb[:, t % 2, 28:29],
                    op0=OP.subtract, op1=OP.mult))
                bld.put("vector", lambda e: e.drain().then_inc(s_zap.h, 1))
                s_zap.bump(1)

            def emit_z_tr(sbt, ti):
                t = sbt * 4 + ti
                u = ring_use[0]
                z_ruse.append(u)
                bld.wait("tensor", s_zap, 4 * sbt + ti + zap_base + 1)
                bld.wait("tensor", s_rv, u)
                for c in range(HC):
                    def tr(e, t=t, c=c, u=u):
                        e.transpose(ffn.ring[:, 0, c, :],
                                    ffn.zst[:, t % 2, c * 128:(c + 1) * 128], idb_sb[:])
                    bld.put("tensor", tr)
                bld.put("tensor", lambda e: e.drain().then_inc(s_rw.h, 1))
                s_rw.bump(1)
                ring_use[0] += 1
                bld.wait("vector", s_rw, u + 1)
                if sbt >= 2 and ti == 0:
                    bld.wait("vector", s_m1, m1_base + NT1 * (sbt - 1))
                bld.put("vector", lambda e, sbt=sbt, ti=ti, u=u: e.tensor_copy(
                    ffn.zt[:, sbt % 2, :, ti * 128:(ti + 1) * 128], ffn.ring[:, 0, :, :]
                ))
                bld.put("vector", lambda e: e.drain().then_inc(s_rv.h, 1))
                zevs.setdefault(sbt, []).append(s_rv.bump(1))

            zap_base = s_zap.n
            emit_z_ln(0)
            emit_z_ln(1)
            emit_z_tr(0, 0)
            emit_z_ln(2)
            emit_z_tr(0, 1)
            emit_z_ln(3)
            emit_z_tr(0, 2)
            emit_z_tr(0, 3)

            for sb in range(nsb):
                LEAD = 2
                pre = pass1 and gen and sb == 0
                for jj in range(NT1 + (LEAD if pre else 0)):
                    if pre and jj < NT1:
                        emit_w0_slab(jj)
                    j = jj - (LEAD if pre else 0)
                    if j < 0 or j >= NT1:
                        continue
                    gj = NT1 * sb + j
                    if pre:
                        bld.wait("sync", s_w0wr, 16 * (j + 1))
                    if gj >= 3:
                        bld.wait("sync", s_m1, m1_base + gj - 2)
                    bld.put("sync", lambda e, j=j: e.dma_start(
                        ffn.w0b[:, j % 3, :, :], w0t[j].rearrange("c h n -> h c n")
                    ).then_inc(s_w0ld.h, 16))
                    s_w0ld.bump(16)
                    bld.wait("tensor", s_w0ld, s_w0ld.n)
                    if j == 0:
                        bld.wait("tensor", s_rv, zevs[sb][-1])
                    if gj >= 3:
                        bld.wait("tensor", s_gel, gel_base + gj - 2)
                    for c in range(HC):
                        st_f = c == 0
                        last = c == HC - 1

                        def mm(e, sb=sb, j=j, c=c, st_f=st_f, last=last):
                            e.matmul(ffn.dps[:, j % 3, :], ffn.w0b[:, j % 3, c, :],
                                     ffn.zt[:, sb % 2, c, :], start=st_f, stop=last)
                        bld.put("tensor", mm)
                    bld.put("tensor", lambda e: e.drain().then_inc(s_m1.h, 1))
                    s_m1.bump(1)
                    bld.wait("scalar", s_m1, m1_base + gj + 1)
                    if sb > 0 and j == 0:
                        bld.wait("scalar", s_m2q, m2q_base + 4 * HC * sb)
                    bld.put("scalar", lambda e, j=j: e.activation(
                        ffn.dt[:, j, :], ffn.dps[:, j % 3, :], AF.Gelu,
                        bias=b0sb[:, j:j + 1], scale=1.0))
                    bld.put("scalar", lambda e: e.drain().then_inc(s_gel.h, 1))
                    s_gel.bump(1)

                do_w1pre = pre
                for hh in range(HC + (1 if do_w1pre else 0)):
                    if do_w1pre and hh < HC:
                        emit_w1_quarters(hh)
                    h = hh - (1 if do_w1pre else 0)
                    if h < 0 or h >= HC:
                        continue
                    gh = HC * sb + h
                    w1ld_vals = [0] * 4
                    for q in range(4):
                        if do_w1pre:
                            bld.wait("sync", s_w1wr, 16 * (4 * h + q + 1))
                        gq = 4 * gh + q
                        if gq >= 3:
                            bld.wait("sync", s_m2q, m2q_base + gq - 2)
                        bld.put("sync", lambda e, h=h, q=q, sl=gq % 3: e.dma_start(
                            ffn.w1b[:, sl, :, :],
                            w1t[h, q * 16:(q + 1) * 16].rearrange("c n h2 -> n c h2")
                        ).then_inc(s_w1ld.h, 16))
                        w1ld_vals[q] = s_w1ld.bump(16)
                    if h == 0:
                        bld.wait("tensor", s_gel, gel_base + NT1 * (sb + 1))
                    if gh >= 3:
                        bld.wait("tensor", s_ep, ep_base + gh - 2)
                    for c in range(NT1):
                        st_f = c == 0
                        last = c == NT1 - 1
                        lastq = (c % 16) == 15
                        if c % 16 == 0:
                            bld.wait("tensor", s_w1ld, w1ld_vals[c // 16])

                        def mm(e, h=h, c=c, st_f=st_f, last=last, lastq=lastq, gh=gh):
                            sl = (4 * gh + c // 16) % 3
                            i = e.matmul(ffn.etps[:, h % 3, :], ffn.w1b[:, sl, c % 16, :],
                                         ffn.dt[:, c, :], start=st_f, stop=last)
                            if lastq and not last:
                                i.then_inc(s_m2q.h, 1)
                        bld.put("tensor", mm)
                    bld.put("tensor", lambda e: e.drain().then_inc(s_m2q.h, 1))
                    s_m2q.bump(4)
                    bld.wait("vector", s_m2q, m2q_base + 4 * (gh + 1))
                    if sb > 0 and h == 0:
                        bld.wait("vector", s_rw, t_ruse[-1] + 1)
                    bld.put("vector", lambda e, h=h: e.tensor_scalar(
                        ffn.et[:, h, :], ffn.etps[:, h % 3, :], 0.5, hb1sb[:, h:h + 1],
                        op0=OP.mult, op1=OP.add))
                    bld.put("vector", lambda e: e.drain().then_inc(s_ep.h, 1))
                    s_ep.bump(1)
                    if sb + 1 < nsb and h % 4 == 3:
                        emit_z_ln(4 * (sb + 1) + h // 4)
                        if h >= 7:
                            emit_z_tr(sb + 1, h // 4 - 1)

                if sb + 1 < nsb:
                    emit_z_tr(sb + 1, 3)

                for ti in range(4):
                    t = sb * 4 + ti
                    row = t * 128
                    if t >= 2:
                        bld.wait("sync", s_rv, tt_rv[t - 2])
                    bld.put("sync", lambda e, row=row, t=t: e.dma_start(
                        ffn.xr[:, t % 2, :], src[row:row + 128, :]).then_inc(s_xr.h, 16))
                    s_xr.bump(16)
                    u = ring_use[0]
                    t_ruse.append(u)
                    if ti == 0:
                        bld.wait("tensor", s_ep, ep_base + HC * (sb + 1))
                    bld.wait("tensor", s_rv, u)
                    for c in range(HC):
                        def tr(e, ti=ti, c=c, u=u):
                            e.transpose(ffn.ring[:, 0, c, :],
                                        ffn.et[:, c, ti * 128:(ti + 1) * 128], idb_sb[:])
                        bld.put("tensor", tr)
                    bld.put("tensor", lambda e: e.drain().then_inc(s_rw.h, 1))
                    s_rw.bump(1)
                    ring_use[0] += 1
                    bld.wait("vector", s_rw, u + 1)
                    bld.wait("vector", s_xr, s_xr.n)
                    if t >= 2:
                        bld.wait("vector", s_dwr, 16 * (t - 1))
                    bld.put("vector", lambda e, t=t, u=u: e.tensor_tensor(
                        ffn.ost[:, t % 2, :], ffn.xr[:, t % 2, :],
                        ffn.ring[:, 0, :, :].rearrange("p c h -> p (c h)"), op=OP.add
                    ))
                    bld.put("vector", lambda e: e.drain().then_inc(s_rv.h, 1))
                    tt_rv.append(s_rv.bump(1))
                    bld.wait("scalar", s_rv, tt_rv[-1])
                    bld.put("scalar", lambda e, row=row, t=t: e.dma_start(
                        dst[row:row + 128, :], ffn.ost[:, t % 2, :]).then_inc(s_dwr.h, 16))
                    s_dwr.bump(16)
            if pass1:
                pass1_preproc_done[0] = True

        class _NS:
            pass

        def ffn_scope(fctx, sfx, pass1):
            fen = fctx.enter_context
            f = _NS()
            f.zt = fen(nc.sbuf_tensor("zt" + sfx, [128, 2, HC, SB], BF16))
            f.dt = fen(nc.sbuf_tensor("dt" + sfx, [128, NT1, SB], BF16))
            f.et = fen(nc.sbuf_tensor("et" + sfx, [128, HC, SB], BF16))
            f.w0b = fen(nc.sbuf_tensor("w0b" + sfx, [128, 3, HC, 128], BF16))
            f.w1b = fen(nc.sbuf_tensor("w1b" + sfx, [128, 3, 16, 128], BF16))
            io_dt = F32 if pass1 else BF16
            f.xa = fen(nc.sbuf_tensor("xa" + sfx, [128, 2, E], io_dt))
            f.xr = fen(nc.sbuf_tensor("xr" + sfx, [128, 2, E], io_dt))
            f.zst = fen(nc.sbuf_tensor("zst" + sfx, [128, 2, E], BF16))
            f.ost = fen(nc.sbuf_tensor("ost" + sfx, [128, 2, E], BF16 if pass1 else F32))
            f.lnb = fen(nc.sbuf_tensor("lnb" + sfx, [128, 2, 32], F32))
            if pass1:
                f.w0sl = fen(nc.sbuf_tensor("w0sl", [128, 2, E], BF16))
                f.w1sl = fen(nc.sbuf_tensor("w1sl", [128, 4096], BF16))
                f.pst = fen(nc.sbuf_tensor("pstg", [128, 16, 128], BF16))
            f.dps = fen(nc.psum_tensor("dps" + sfx, [128, 3, SB], F32))
            f.etps = fen(nc.psum_tensor("etps" + sfx, [128, 3, SB], F32))
            f.ring = fen(nc.psum_tensor("ring" + sfx, [128, 1, HC, 128], BF16))
            return f

        if "ffn1" in stages:
            with ExitStack() as fctx:
                ffn = ffn_scope(fctx, "1", True)
                gen_ffn(True, ffn)
                bld.barrier(s_stg)
                with nc.Block() as blk:
                    bld.flush(blk)

        def gen_attn(b, at):
            R = b * SEQ
            s_hld = S("hld")
            s_wldq = S("wldq")
            s_ytwr = S("ytwr")
            s_qwr = S("qwr")
            s_vtwr = S("vtwr")
            s_atwr = S("atwr")
            s_otwr = S("otwr")
            s_stl = S("stl")
            s_pmm = S("pmm")
            s_pev = S("pev")
            s_lnv = S("lnv")
            s_sq = S("sq")
            s_zap = S("zap")
            s_sred = S("sred")
            s_exp = S("exp")
            s_asc = S("asc")
            s_gev = S("gev")
            s_h2wr = S("h2wr")
            s_h1wr = sems["h1wr"]
            a_ruse, c_ruse, e_ruse, g_rv = [], [], [], []

            def wload(dstt, w_ap):
                for k in range(4):
                    bld.put("gpsimd", lambda e, w_ap=w_ap, k=k, dstt=dstt: e.dma_start(
                        dstt[:, 4 * k:4 * (k + 1), :],
                        w_ap.rearrange("(c p) f -> p c f", p=128)[:, 4 * k:4 * (k + 1), :]
                    ).then_inc(s_wldq.h, 16))
                    s_wldq.bump(16)
                return s_wldq.n

            half_ctr = [0]
            half_done = {}

            def tile_halved(i, rhs_buf, evac, done_sem):
                for half in range(2):
                    hh = half_ctr[0]
                    half_ctr[0] += 1
                    if hh >= 2:
                        bld.wait("tensor", done_sem, half_done[hh - 2])
                    for cg in (2 * half, 2 * half + 1):
                        for c in range(HC):
                            st_f = c == 0
                            def mm(e, i=i, cg=cg, c=c, st_f=st_f, hh=hh, rhs_buf=rhs_buf):
                                e.matmul(at.pbig[:, hh % 2, (cg % 2) * 512:(cg % 2 + 1) * 512],
                                         at.stt[:, i % 3, c, :],
                                         rhs_buf[:, c, cg * 512:(cg + 1) * 512],
                                         start=st_f, stop=(c == HC - 1))
                            bld.put("tensor", mm)
                    bld.put("tensor", lambda e: e.drain().then_inc(s_pmm.h, 1))
                    s_pmm.bump(1)
                    evac(half, hh, s_pmm.n)

            psum_guard()
            bld.wait("sync", s_h1wr, 16 * HC * (b + 1))
            wq_done = wload(at.W_a, wq)
            for i in range(HC):
                if i >= 2:
                    bld.wait("sync", s_zap, s_zap.n - 1)
                bld.put("sync", lambda e, i=i: e.dma_start(
                    at.hst[:, i % 2, :], h1[R + i * 128:R + (i + 1) * 128, :]).then_inc(s_hld.h, 16))
                s_hld.bump(16)
                bld.wait("vector", s_hld, s_hld.n)
                for k in range(4):
                    bld.put("vector", lambda e, i=i, k=k: e.bn_stats(
                        at.lnb[:, i % 2, 6 * k:6 * (k + 1)], at.hst[:, i % 2, 512 * k:512 * (k + 1)]))
                bld.put("vector", lambda e: e.drain())
                bld.put("vector", lambda e, i=i: e.bn_aggr(at.lnb[:, i % 2, 24:26], at.lnb[:, i % 2, 0:24]))
                bld.put("vector", lambda e: e.drain())
                bld.put("vector", lambda e, i=i: e.tensor_scalar(
                    at.lnb[:, i % 2, 26:27], at.lnb[:, i % 2, 25:26], EPS, None, op0=OP.add
                ))
                bld.put("vector", lambda e: e.drain().then_inc(s_lnv.h, 1))
                s_lnv.bump(1)
                bld.wait("scalar", s_lnv, s_lnv.n)
                bld.put("scalar", lambda e, i=i: e.activation(
                    at.lnb[:, i % 2, 27:28], at.lnb[:, i % 2, 26:27], AF.Sqrt))
                bld.put("scalar", lambda e: e.drain().then_inc(s_sq.h, 1))
                s_sq.bump(1)
                bld.wait("vector", s_sq, s_sq.n)
                bld.put("vector", lambda e, i=i: e.reciprocal(at.lnb[:, i % 2, 28:29], at.lnb[:, i % 2, 27:28]))
                bld.put("vector", lambda e: e.drain())
                if i >= 2:
                    bld.wait("vector", s_rw, a_ruse[i - 2] + 1)
                bld.put("vector", lambda e, i=i: e.tensor_scalar(
                    at.ast[:, i % 2, :], at.hst[:, i % 2, :],
                    at.lnb[:, i % 2, 24:25], at.lnb[:, i % 2, 28:29],
                    op0=OP.subtract, op1=OP.mult))
                bld.put("vector", lambda e: e.drain().then_inc(s_zap.h, 1))
                s_zap.bump(1)
                u = ring_use[0]
                a_ruse.append(u)
                bld.wait("tensor", s_zap, s_zap.n)
                bld.wait("tensor", s_rv, u - 1)
                for c in range(HC):
                    last = c == HC - 1

                    def tr(e, i=i, c=c, u=u, last=last):
                        ins = e.transpose(at.ring2[:, u % 2, c, :],
                                          at.ast[:, i % 2, c * 128:(c + 1) * 128], idb_sb[:])
                    bld.put("tensor", tr)
                bld.put("tensor", lambda e: e.drain().then_inc(s_rw.h, 1))
                s_rw.bump(1)
                ring_use[0] += 1
                bld.wait("vector", s_rw, u + 1)
                if i >= 2:
                    bld.wait("vector", s_ytwr, s_ytwr.n - 16)
                bld.put("vector", lambda e, i=i, u=u: e.tensor_copy(
                    at.evst[:, i % 2, :], at.ring2[:, u % 2, :, :].rearrange("p c s -> p (c s)")
                ))
                bld.put("vector", lambda e: e.drain().then_inc(s_rv.h, 1))
                rv_val = s_rv.bump(1)
                bld.wait("scalar", s_rv, rv_val)
                bld.put("scalar", lambda e, i=i: e.dma_start(
                    ytd[i], at.evst[:, i % 2, :]).then_inc(s_ytwr.h, 16))
                s_ytwr.bump(16)
            bld.barrier(s_stg)

            def proj_phase(w_buf, out_kind, wld_target):
                psum_guard()
                half_ctr[0] = 0
                half_done.clear()
                pmm_tile = {}
                for i in range(HC):
                    if i >= 3:
                        bld.wait("sync", s_pmm, s_pmm.n - 4)
                    bld.put("sync", lambda e, i=i: e.dma_start(
                        at.stt[:, i % 3, :, :], ytd[i].rearrange("p (c s) -> p c s", c=HC)
                    ).then_inc(s_stl.h, 16))
                    s_stl.bump(16)
                    bld.wait("tensor", s_stl, s_stl.n)
                    if i == 0:
                        bld.wait("tensor", s_wldq, wld_target)

                    def evac(half, hh, pmm_val, i=i):
                        bld.wait("scalar", s_pmm, pmm_val)
                        if out_kind == "q" and half == 0 and i >= 2:
                            bld.wait("scalar", s_qwr, s_qwr.n - 16)
                        dest = at.evst[:, i % 2, 1024 * half:1024 * (half + 1)] if out_kind == "q" \
                            else at.W_b[:, i, 1024 * half:1024 * (half + 1)]
                        bld.put("scalar", lambda e, dest=dest, hh=hh: e.copy(
                            dest, at.pbig[:, hh % 2, :]))
                        bld.put("scalar", lambda e: e.drain().then_inc(s_pev.h, 1))
                        half_done[hh] = s_pev.bump(1)
                        if out_kind == "q" and half == 1:
                            bld.wait("scalar", s_pev, half_done[hh])
                            bld.put("scalar", lambda e, i=i: e.dma_start(
                                qd[i * 128:(i + 1) * 128, :], at.evst[:, i % 2, :]
                            ).then_inc(s_qwr.h, 16))
                            s_qwr.bump(16)

                    tile_halved(i, w_buf, evac, s_pev)
                    pmm_tile[i] = s_pmm.n

            wv_done = wload(at.W_b, wv)
            proj_phase(at.W_a, "q", wq_done)
            bld.barrier(s_stg)

            wk_done = wload(at.W_a, wk)
            psum_guard()
            half_ctr[0] = 0
            half_done.clear()
            for i in range(HC):
                if i >= 3:
                    bld.wait("sync", s_pmm, s_pmm.n - 4)
                bld.put("sync", lambda e, i=i: e.dma_start(
                    at.stt[:, i % 3, :, :], ytd[i].rearrange("p (c s) -> p c s", c=HC)
                ).then_inc(s_stl.h, 16))
                s_stl.bump(16)
                bld.wait("tensor", s_stl, s_stl.n)
                if i == 0:
                    bld.wait("tensor", s_wldq, wv_done)

                def evac(half, hh, pmm_val, i=i):
                    bld.wait("scalar", s_pmm, pmm_val)
                    if half == 0 and i >= 2:
                        bld.wait("scalar", s_rw, c_ruse[i - 2] + 1)
                    bld.put("scalar", lambda e, i=i, hh=hh, half=half: e.copy(
                        at.evst[:, i % 2, 1024 * half:1024 * (half + 1)], at.pbig[:, hh % 2, :]))
                    bld.put("scalar", lambda e: e.drain().then_inc(s_pev.h, 1))
                    half_done[hh] = s_pev.bump(1)

                tile_halved(i, at.W_b, evac, s_pev)
                u = ring_use[0]
                c_ruse.append(u)
                bld.wait("tensor", s_pev, half_done[half_ctr[0] - 1])
                bld.wait("tensor", s_rv, u - 1)
                for c in range(HC):
                    def tr(e, i=i, c=c, u=u):
                        e.transpose(at.ring2[:, u % 2, c, :],
                                    at.evst[:, i % 2, c * 128:(c + 1) * 128], idb_sb[:])
                    bld.put("tensor", tr)
                bld.put("tensor", lambda e: e.drain().then_inc(s_rw.h, 1))
                s_rw.bump(1)
                ring_use[0] += 1
                bld.wait("vector", s_rw, u + 1)
                if i >= 2:
                    bld.wait("vector", s_vtwr, s_vtwr.n - 16)
                bld.put("vector", lambda e, i=i, u=u: e.tensor_copy(
                    at.ast[:, i % 2, :], at.ring2[:, u % 2, :, :].rearrange("p c s -> p (c s)")
                ))
                bld.put("vector", lambda e: e.drain().then_inc(s_rv.h, 1))
                rv_val = s_rv.bump(1)
                bld.wait("scalar", s_rv, rv_val)
                bld.put("scalar", lambda e, i=i: e.dma_start(
                    vtd.rearrange("(c p) s -> p c s", p=128)[:, :, i * 128:(i + 1) * 128],
                    at.ast[:, i % 2, :].rearrange("p (c s) -> p c s", c=HC)).then_inc(s_vtwr.h, 16))
                s_vtwr.bump(16)
            bld.barrier(s_stg)

            proj_phase(at.W_a, "k", wk_done)
            bld.barrier(s_stg)

            for k in range(4):
                bld.wait("gpsimd", s_vtwr, s_vtwr.n)
                bld.put("gpsimd", lambda e, k=k: e.dma_start(
                    at.W_a[:, 4 * k:4 * (k + 1), :],
                    vtd.rearrange("(c p) s -> p c s", p=128)[:, 4 * k:4 * (k + 1), :]
                ).then_inc(s_wldq.h, 16))
                s_wldq.bump(16)
            vt_done = s_wldq.n
            psum_guard()
            for i in range(HC):
                if i >= 3:
                    bld.wait("sync", s_pmm, s_pmm.n - 2)
                if i == 0:
                    bld.wait("sync", s_qwr, s_qwr.n)
                bld.put("sync", lambda e, i=i: e.dma_start(
                    at.stt[:, i % 3, :, :],
                    qd.rearrange("(c p) e -> p c e", p=128)[:, :, i * 128:(i + 1) * 128]
                ).then_inc(s_stl.h, 16))
                s_stl.bump(16)
                bld.wait("tensor", s_stl, s_stl.n)
                if i >= 1:
                    bld.wait("tensor", s_exp, s_exp.n)
                for cg in range(4):
                    for c in range(HC):
                        st_f = c == 0
                        last = (c == HC - 1) and (cg == 3)

                        def mm(e, i=i, cg=cg, c=c, st_f=st_f, last=last):
                            ins = e.matmul(at.pbig[:, cg // 2, (cg % 2) * 512:(cg % 2 + 1) * 512],
                                           at.stt[:, i % 3, c, :],
                                           at.W_b[:, c, cg * 512:(cg + 1) * 512],
                                           start=st_f, stop=(c == HC - 1))
                        bld.put("tensor", mm)
                bld.put("tensor", lambda e: e.drain().then_inc(s_pmm.h, 1))
                s_pmm.bump(1)
                bld.wait("vector", s_pmm, s_pmm.n)
                bld.put("vector", lambda e, i=i: e.reduce_max(
                    at.lnb[:, i % 2, 29:30], at.pbig[:, :, :].rearrange("p a b -> p (a b)"), axis=AX))
                bld.put("vector", lambda e: e.drain())
                bld.put("vector", lambda e, i=i: e.tensor_scalar(
                    at.lnb[:, i % 2, 30:31], at.lnb[:, i % 2, 29:30], -0.125, None, op0=OP.mult
                ))
                bld.put("vector", lambda e: e.drain().then_inc(s_sred.h, 1))
                s_sred.bump(1)
                bld.wait("scalar", s_sred, s_sred.n)
                if i >= 2:
                    bld.wait("scalar", s_asc, s_asc.n - 1)
                bld.put("scalar", lambda e, i=i: e.activation(
                    at.pexp[:, i % 2, :], at.pbig[:, :, :].rearrange("p a b -> p (a b)"),
                    AF.Exp, bias=at.lnb[:, i % 2, 30:31],
                    scale=0.125, accum_out=at.lnb[:, i % 2, 31:32]))
                bld.put("scalar", lambda e: e.drain().then_inc(s_exp.h, 1))
                s_exp.bump(1)
                bld.wait("vector", s_exp, s_exp.n)
                bld.put("vector", lambda e, i=i: e.reciprocal(at.lnb[:, i % 2, 28:29], at.lnb[:, i % 2, 31:32]))
                bld.put("vector", lambda e: e.drain())
                if i >= 2:
                    bld.wait("vector", s_rw, e_ruse[i - 2] + 1)
                bld.put("vector", lambda e, i=i: e.tensor_scalar(
                    at.ast[:, i % 2, :], at.pexp[:, i % 2, :], at.lnb[:, i % 2, 28:29], None, op0=OP.mult
                ))
                bld.put("vector", lambda e: e.drain().then_inc(s_asc.h, 1))
                s_asc.bump(1)
                u = ring_use[0]
                e_ruse.append(u)
                bld.wait("tensor", s_asc, s_asc.n)
                bld.wait("tensor", s_rv, u - 1)
                for c in range(HC):
                    last = c == HC - 1

                    def tr(e, i=i, c=c, u=u, last=last):
                        ins = e.transpose(at.ring2[:, u % 2, c, :],
                                          at.ast[:, i % 2, c * 128:(c + 1) * 128], idb_sb[:])
                    bld.put("tensor", tr)
                bld.put("tensor", lambda e: e.drain().then_inc(s_rw.h, 1))
                s_rw.bump(1)
                ring_use[0] += 1
                bld.wait("vector", s_rw, u + 1)
                if i >= 2:
                    bld.wait("vector", s_atwr, s_atwr.n - 16)
                bld.put("vector", lambda e, i=i, u=u: e.tensor_copy(
                    at.evst[:, i % 2, :], at.ring2[:, u % 2, :, :].rearrange("p c s -> p (c s)")
                ))
                bld.put("vector", lambda e: e.drain().then_inc(s_rv.h, 1))
                rv_val = s_rv.bump(1)
                bld.wait("scalar", s_rv, rv_val)
                bld.put("scalar", lambda e, i=i: e.dma_start(
                    atd[i], at.evst[:, i % 2, :]).then_inc(s_atwr.h, 16))
                s_atwr.bump(16)
            bld.barrier(s_stg)

            wo_done = wload(at.W_b, wo)
            psum_guard()
            half_ctr[0] = 0
            half_done.clear()
            atwr_base = s_atwr.n - 16 * HC
            for i in range(HC):
                if i >= 3:
                    bld.wait("sync", s_pmm, s_pmm.n - 4)
                bld.wait("sync", s_atwr, atwr_base + 16 * (i + 1))
                bld.put("sync", lambda e, i=i: e.dma_start(
                    at.stt[:, i % 3, :, :], atd[i].rearrange("p (c s) -> p c s", c=HC)
                ).then_inc(s_stl.h, 16))
                s_stl.bump(16)
                bld.wait("tensor", s_stl, s_stl.n)
                if i == 0:
                    bld.wait("tensor", s_wldq, vt_done)

                def evac(half, hh, pmm_val, i=i):
                    bld.wait("scalar", s_pmm, pmm_val)
                    if half == 0 and i >= 2:
                        bld.wait("scalar", s_otwr, s_otwr.n - 16)
                    bld.put("scalar", lambda e, i=i, hh=hh, half=half: e.copy(
                        at.evst[:, i % 2, 1024 * half:1024 * (half + 1)], at.pbig[:, hh % 2, :]))
                    bld.put("scalar", lambda e: e.drain().then_inc(s_pev.h, 1))
                    half_done[hh] = s_pev.bump(1)
                    if half == 1:
                        bld.put("scalar", lambda e, i=i: e.dma_start(
                            otd[i * 128:(i + 1) * 128, :], at.evst[:, i % 2, :]).then_inc(s_otwr.h, 16))
                        s_otwr.bump(16)

                tile_halved(i, at.W_a, evac, s_pev)
            bld.barrier(s_stg)

            psum_guard()
            half_ctr[0] = 0
            half_done.clear()
            for i in range(HC):
                if i >= 3:
                    bld.wait("sync", s_pmm, s_pmm.n - 4)
                if i >= 2:
                    bld.wait("sync", s_gev, g_rv[2 * i - 3])
                if i == 0:
                    bld.wait("sync", s_otwr, s_otwr.n)
                bld.put("sync", lambda e, i=i: e.dma_start(
                    at.stt[:, i % 3, :, :],
                    otd.rearrange("(c p) s -> p c s", p=128)[:, :, i * 128:(i + 1) * 128]
                ).then_inc(s_stl.h, 16))
                s_stl.bump(16)
                bld.put("sync", lambda e, i=i: e.dma_start(
                    at.hst[:, i % 2, :], h1[R + i * 128:R + (i + 1) * 128, :]).then_inc(s_hld.h, 16))
                s_hld.bump(16)
                bld.wait("tensor", s_stl, s_stl.n)
                if i == 0:
                    bld.wait("tensor", s_wldq, wo_done)
                hld_val = s_hld.n

                def evac(half, hh, pmm_val, i=i, hld_val=hld_val):
                    bld.wait("vector", s_pmm, pmm_val)
                    bld.wait("vector", s_hld, hld_val)
                    if half == 0 and i >= 2:
                        bld.wait("vector", s_h2wr, s_h2wr.n - 16)
                    bld.put("vector", lambda e, i=i, hh=hh, half=half: e.tensor_tensor(
                        at.h2st[:, i % 2, 1024 * half:1024 * (half + 1)],
                        at.hst[:, i % 2, 1024 * half:1024 * (half + 1)],
                        at.pbig[:, hh % 2, :], op=OP.add))
                    bld.put("vector", lambda e: e.drain().then_inc(s_gev.h, 1))
                    half_done[hh] = s_gev.bump(1)
                    g_rv.append(half_done[hh])
                    if half == 1:
                        bld.wait("scalar", s_gev, half_done[hh])
                        bld.put("scalar", lambda e, i=i: e.dma_start(
                            h2[R + i * 128:R + (i + 1) * 128, :], at.h2st[:, i % 2, :]
                        ).then_inc(s_h2wr.h, 16))
                        s_h2wr.bump(16)

                tile_halved(i, at.W_b, evac, s_gev)

        for b, stg in ((0, "attn0"), (1, "attn1")):
            if stg not in stages:
                continue
            with ExitStack() as actx:
                aen = actx.enter_context
                sfx = "a%d" % b
                at = _NS()
                at.W_a = aen(nc.sbuf_tensor("W_a" + sfx, [128, HC, E], BF16))
                at.W_b = aen(nc.sbuf_tensor("W_b" + sfx, [128, HC, E], BF16))
                at.stt = aen(nc.sbuf_tensor("stt" + sfx, [128, 3, HC, 128], BF16))
                at.evst = aen(nc.sbuf_tensor("evst" + sfx, [128, 2, E], BF16))
                at.pexp = aen(nc.sbuf_tensor("pexp" + sfx, [128, 2, E], BF16))
                at.ast = aen(nc.sbuf_tensor("ast" + sfx, [128, 2, E], BF16))
                at.hst = aen(nc.sbuf_tensor("hst" + sfx, [128, 2, E], BF16))
                at.h2st = aen(nc.sbuf_tensor("h2st" + sfx, [128, 2, E], BF16))
                at.lnb = aen(nc.sbuf_tensor("lnb" + sfx, [128, 2, 32], F32))
                at.pbig = aen(nc.psum_tensor("pbig" + sfx, [128, 2, 1024], F32))
                at.ring2 = aen(nc.psum_tensor("ring2" + sfx, [128, 2, HC, 128], BF16))
                gen_attn(b, at)
                bld.barrier(s_stg)
                with nc.Block() as blk:
                    bld.flush(blk)

        if "ffn3" in stages:
            with ExitStack() as fctx:
                ffn = ffn_scope(fctx, "3", False)
                gen_ffn(False, ffn)
                bld.barrier(s_stg)
                with nc.Block() as blk:
                    bld.flush(blk)

        final_waits = []
        if "ffn3" in stages:
            final_waits.append(("ywr", sems["ywr"].n))
        for nm in debug_out:
            wrs = {"h1": "h1wr", "h2": "h2wr"}.get(nm)
            if wrs and wrs in sems:
                final_waits.append((wrs, sems[wrs].n))
        with nc.Block() as blk:
            @blk.sync
            def _(sy):
                for nm, v in final_waits:
                    sy.wait_ge(sems[nm].h, v)

    return nc


def _prep_inputs(L0_attn_x, L0_attn_Wq, L0_attn_Wk, L0_attn_Wv, L0_attn_Wo,
                 L0_ffn_W0, L0_ffn_bias0, L0_ffn_W1, L0_ffn_bias1):
    f = lambda a: np.ascontiguousarray(np.asarray(a, dtype=np.float32))
    xs = f(L0_attn_x)
    common = dict(
        wq=f(L0_attn_Wq), wk=f(L0_attn_Wk), wv=f(L0_attn_Wv), wo=f(L0_attn_Wo),
        w0=f(L0_ffn_W0), w1=f(L0_ffn_W1),
        b0r=f(L0_ffn_bias0).reshape(NT1, 128), b1r=f(L0_ffn_bias1).reshape(HC, 128),
        idf=np.eye(128, dtype=np.float32), idb=np.eye(128, dtype=ml_dtypes.bfloat16),
    )
    maps = []
    for c in range(NCORES):
        m = dict(common)
        m["x"] = np.ascontiguousarray(xs[2 * c:2 * c + 2].reshape(TPC, E))
        maps.append(m)
    return maps


def kernel(L0_attn_x, L0_attn_Wq, L0_attn_Wk, L0_attn_Wv, L0_attn_Wo,
           L0_ffn_W0, L0_ffn_bias0, L0_ffn_W1, L0_ffn_bias1):
    maps = _prep_inputs(L0_attn_x, L0_attn_Wq, L0_attn_Wk, L0_attn_Wv, L0_attn_Wo,
                        L0_ffn_W0, L0_ffn_bias0, L0_ffn_W1, L0_ffn_bias1)
    nc = _build()
    res = run_bass_kernel_spmd(nc, maps, core_ids=list(range(NCORES)))
    out = np.concatenate(
        [np.asarray(r["y"], dtype=np.float32).reshape(2, SEQ, E) for r in res.results], axis=0)
    return out


# version 12 (final)
# speedup vs baseline: 1.0330x; 1.0330x over previous
import numpy as np
import ml_dtypes
from contextlib import ExitStack

import concourse.bass as bass
import concourse.mybir as mybir
from concourse.alu_op_type import AluOpType as OP
from concourse.bass_utils import run_bass_kernel_spmd

F32 = mybir.dt.float32
BF16 = mybir.dt.bfloat16
AF = mybir.ActivationFunctionType
AX = mybir.AxisListType.X

NCORES = 8
SEQ = 2048
E = 2048
NF = 8192
TPC = 4096
SB = 512
NSB = TPC // SB
NT1 = NF // 128
HC = E // 128
EPS = 1e-5
DEBUG_SB_BARRIER = False
DEBUG_LN = False

ENGS = ("tensor", "vector", "scalar", "sync", "gpsimd")


class _Sem:
    __slots__ = ("h", "n")

    def __init__(self, h):
        self.h = h
        self.n = 0

    def bump(self, k=1):
        self.n += k
        return self.n


class _B:

    def __init__(self, nc):
        self.nc = nc
        self.q = {e: [] for e in ENGS}

    def put(self, eng, fn):
        self.q[eng].append(fn)

    def wait(self, eng, sem, val):
        if val > 0:
            self.q[eng].append(lambda en, s=sem.h, v=val: en.wait_ge(s, v))

    def barrier(self, sem):
        for e in ENGS:
            self.q[e].append(lambda en, s=sem.h: en.nop().then_inc(s, 1))
        sem.n += len(ENGS)
        for e in ENGS:
            self.wait(e, sem, sem.n)

    def flush(self, block):
        for ename in ENGS:
            lst = self.q[ename]

            def run(eng, lst=lst):
                for fn in lst:
                    fn(eng)

            getattr(block, ename)(run)
        self.q = {e: [] for e in ENGS}


def _build(stages=("ffn1", "attn0", "attn1", "ffn3"), debug_out=(), nsb=NSB,
           ffn_phases=("z", "m1", "m2", "t")):
    nc = bass.Bass("TRN2", target_bir_lowering=False)

    def dram(name, shape, dtype):
        kd = "ExternalOutput" if name in debug_out else "Internal"
        return nc.dram_tensor(name, shape, dtype, kind=kd).ap()

    x = nc.dram_tensor("x", [TPC, E], F32, kind="ExternalInput").ap()
    wq = nc.dram_tensor("wq", [E, E], F32, kind="ExternalInput").ap()
    wk = nc.dram_tensor("wk", [E, E], F32, kind="ExternalInput").ap()
    wv = nc.dram_tensor("wv", [E, E], F32, kind="ExternalInput").ap()
    wo = nc.dram_tensor("wo", [E, E], F32, kind="ExternalInput").ap()
    w0 = nc.dram_tensor("w0", [NF, E], F32, kind="ExternalInput").ap()
    w1 = nc.dram_tensor("w1", [E, NF], F32, kind="ExternalInput").ap()
    b0r = nc.dram_tensor("b0r", [NT1, 128], F32, kind="ExternalInput").ap()
    b1r = nc.dram_tensor("b1r", [HC, 128], F32, kind="ExternalInput").ap()
    idf = nc.dram_tensor("idf", [128, 128], F32, kind="ExternalInput").ap()
    idb = nc.dram_tensor("idb", [128, 128], BF16, kind="ExternalInput").ap()

    y = nc.dram_tensor("y", [TPC, E], F32, kind="ExternalOutput").ap()

    w0t = dram("w0t", [NT1, HC, 128, 128], BF16)
    w1t = dram("w1t", [HC, NT1, 128, 128], BF16)
    h1 = dram("h1", [TPC, E], BF16)
    h2 = dram("h2", [TPC, E], BF16)
    ytd = dram("ytd", [HC, 128, E], BF16)
    qd = dram("qd", [SEQ, E], BF16)
    vtd = dram("vtd", [E, SEQ], BF16)
    atd = dram("atd", [HC, 128, E], BF16)
    otd = dram("otd", [E, SEQ], BF16)
    dbg = dram("dbg", [32, 128, 8], F32)

    bld = _B(nc)
    sems = {}

    with ExitStack() as ctx:
        en = ctx.enter_context

        def S(name):
            if name not in sems:
                sems[name] = _Sem(en(nc.semaphore(name)))
            return sems[name]

        idb_sb = en(nc.sbuf_tensor("idb_sb", [128, 128], BF16))
        idf_sb = en(nc.sbuf_tensor("idf_sb", [128, 128], F32))
        b0sb = en(nc.sbuf_tensor("b0sb", [128, NT1], F32))
        hb1sb = en(nc.sbuf_tensor("hb1sb", [128, HC], F32))

        def psum_guard():
            for nm in ("gel", "ep", "pev", "exp", "gev", "rv"):
                if nm in sems and sems[nm].n > 0:
                    bld.wait("tensor", sems[nm], sems[nm].n)

        s_stg = S("stg")
        s_rw = S("rw")
        s_rv = S("rv")
        ring_use = [0]

        with ExitStack() as pctx:
            pen = pctx.enter_context
            b0ld = pen(nc.sbuf_tensor("b0ld", [NT1, 128], F32))
            b1ld = pen(nc.sbuf_tensor("b1ld", [HC, 128], F32))
            pb0 = pen(nc.psum_tensor("pb0", [128, NT1], F32))
            pb1 = pen(nc.psum_tensor("pb1", [128, HC], F32))
            s_ini = S("ini")
            s_pp = S("pp")
            bld.put("scalar", lambda e: e.dma_start(idb_sb[:], idb[:, :]).then_inc(s_ini.h, 16))
            bld.put("scalar", lambda e: e.dma_start(idf_sb[:], idf[:, :]).then_inc(s_ini.h, 16))
            bld.put("scalar", lambda e: e.dma_start(b0ld[:], b0r[:, :]).then_inc(s_ini.h, 16))
            bld.put("scalar", lambda e: e.dma_start(b1ld[:], b1r[:, :]).then_inc(s_ini.h, 16))
            s_ini.n = 64
            bld.wait("tensor", s_ini, 64)
            bld.put("tensor", lambda e: e.transpose(pb0[:], b0ld[:], idf_sb[0:NT1, 0:NT1]))
            bld.put("tensor", lambda e: e.transpose(pb1[:], b1ld[:], idf_sb[0:HC, 0:HC]))
            bld.put("tensor", lambda e: e.drain().then_inc(s_pp.h, 1))
            s_pp.n = 1
            bld.wait("vector", s_pp, 1)
            bld.put("vector", lambda e: e.tensor_copy(b0sb[:], pb0[:]))
            bld.put("vector", lambda e: e.tensor_scalar(hb1sb[:], pb1[:], 0.5, None, op0=OP.mult))
            bld.barrier(s_stg)
            with nc.Block() as blk:
                bld.flush(blk)

        pass1_preproc_done = [False]

        def gen_ffn(pass1, ffn):
            src = x if pass1 else h2
            dst = h1 if pass1 else y
            s_xa = S("xa1" if pass1 else "xa3")
            s_xr = S("xr1" if pass1 else "xr3")
            s_lnv = S("lnv")
            s_sq = S("sq")
            s_zap = S("zap")
            s_m1 = S("m1")
            s_gel = S("gel")
            s_m2q = S("m2q")
            s_ep = S("ep")
            s_w0ld = S("w0ld")
            s_w1ld = S("w1ld")
            s_dwr = S("h1wr" if pass1 else "ywr")
            gen = pass1 and not pass1_preproc_done[0]
            if gen:
                s_w0sl = S("w0sl")
                s_w0f = S("w0f")
                s_ppcp = S("ppcp")
                s_w1sl = S("w1sl")
                s_w0wr = S("w0wr")
                s_w1wr = S("w1wr")
            if not pass1 and "h2wr" in sems:
                bld.wait("sync", sems["h2wr"], sems["h2wr"].n)

            psum_guard()
            m1_base = s_m1.n
            m2q_base = s_m2q.n
            gel_base = s_gel.n
            ep_base = s_ep.n
            z_ruse, t_ruse, tt_rv = [], [], []
            pp_w0_ruse, pp_w1_ruse = [], []

            def emit_w0_slab(j):
                if j % 2 == 0:
                    if j >= 2:
                        bld.wait("gpsimd", s_rw, pp_w0_ruse[j - 2] + 1)
                    bld.put("gpsimd", lambda e, j=j: e.dma_start(
                        ffn.w0sl[:, j % 2, :], w0[j * 128:(j + 1) * 128, :]).then_inc(s_w0sl.h, 16))
                    s_w0sl.bump(16)
                    ready = (s_w0sl, s_w0sl.n)
                else:
                    k = (j - 1) // 2
                    sl = k % 2
                    if k < 2:
                        bld.wait("sync", s_zap, zap_base + 4)
                    else:
                        bld.wait("sync", s_ppcp, k - 1)
                    bld.put("sync", lambda e, j=j, sl=sl: e.dma_start(
                        ffn.xa[:, sl, :], w0[j * 128:(j + 1) * 128, :]).then_inc(s_w0f.h, 16))
                    s_w0f.bump(16)
                    bld.wait("vector", s_w0f, s_w0f.n)
                    if j >= 2:
                        bld.wait("vector", s_rw, pp_w0_ruse[j - 2] + 1)
                    bld.put("vector", lambda e, j=j, sl=sl: e.tensor_copy(
                        ffn.w0sl[:, j % 2, :], ffn.xa[:, sl, :]))
                    bld.put("vector", lambda e: e.drain().then_inc(s_ppcp.h, 1))
                    s_ppcp.bump(1)
                    ready = (s_ppcp, s_ppcp.n)
                u = ring_use[0]
                pp_w0_ruse.append(u)
                bld.wait("tensor", ready[0], ready[1])
                bld.wait("tensor", s_rv, u)
                for c in range(HC):
                    def tr(e, j=j, c=c, u=u):
                        e.transpose(ffn.ring[:, 0, c, :],
                                    ffn.w0sl[:, j % 2, c * 128:(c + 1) * 128], idb_sb[:])
                    bld.put("tensor", tr)
                bld.put("tensor", lambda e: e.drain().then_inc(s_rw.h, 1))
                s_rw.bump(1)
                ring_use[0] += 1
                bld.wait("vector", s_rw, u + 1)
                if j >= 1:
                    bld.wait("vector", s_w0wr, 16 * j)
                bld.put("vector", lambda e, j=j, u=u: e.tensor_copy(
                    ffn.pst[:], ffn.ring[:, 0, :, :]))
                bld.put("vector", lambda e: e.drain().then_inc(s_rv.h, 1))
                s_rv.bump(1)
                bld.wait("scalar", s_rv, u + 1)
                bld.put("scalar", lambda e, j=j: e.dma_start(
                    w0t[j].rearrange("c h n -> h c n"), ffn.pst[:]).then_inc(s_w0wr.h, 16))
                s_w0wr.bump(16)

            def emit_w1_slab(h):
                pass

            def emit_w1_quarters(h):
                for q in range(4):
                    gq = h * 4 + q
                    if gq >= 2:
                        bld.wait("gpsimd", s_rw, pp_w1_ruse[gq - 2] + 1)
                    bld.put("gpsimd", lambda e, h=h, q=q: e.dma_start(
                        ffn.w1sl[:, (q % 2) * 2048:(q % 2 + 1) * 2048],
                        w1[h * 128:(h + 1) * 128, q * 2048:(q + 1) * 2048]
                    ).then_inc(s_w1sl.h, 16))
                    s_w1sl.bump(16)
                    u = ring_use[0]
                    pp_w1_ruse.append(u)
                    bld.wait("tensor", s_w1sl, 16 * (gq + 1))
                    bld.wait("tensor", s_rv, u)
                    for cl in range(16):
                        def tr(e, q=q, cl=cl, u=u):
                            e.transpose(ffn.ring[:, 0, cl, :],
                                        ffn.w1sl[:, (q % 2) * 2048 + cl * 128:
                                                 (q % 2) * 2048 + (cl + 1) * 128], idb_sb[:])
                        bld.put("tensor", tr)
                    bld.put("tensor", lambda e: e.drain().then_inc(s_rw.h, 1))
                    s_rw.bump(1)
                    ring_use[0] += 1
                    bld.wait("vector", s_rw, u + 1)
                    if gq >= 1:
                        bld.wait("vector", s_w1wr, 16 * gq)
                    else:
                        bld.wait("vector", s_w0wr, s_w0wr.n)
                    bld.put("vector", lambda e, q=q, u=u: e.tensor_copy(
                        ffn.pst[:], ffn.ring[:, 0, :, :]))
                    bld.put("vector", lambda e: e.drain().then_inc(s_rv.h, 1))
                    s_rv.bump(1)
                    bld.wait("scalar", s_rv, u + 1)
                    bld.put("scalar", lambda e, h=h, q=q: e.dma_start(
                        w1t[h, q * 16:(q + 1) * 16].rearrange("c n h2 -> n c h2"),
                        ffn.pst[:]).then_inc(s_w1wr.h, 16))
                    s_w1wr.bump(16)

            zevs = {}

            def emit_z_ln(t):
                row = t * 128
                if gen and t in (4, 5) and "ppcp" in sems:
                    bld.wait("sync", sems["ppcp"], sems["ppcp"].n)
                if t >= 2:
                    bld.wait("sync", s_zap, s_zap.n - 1)
                bld.put("sync", lambda e, row=row, t=t: e.dma_start(
                    ffn.xa[:, t % 2, :], src[row:row + 128, :]).then_inc(s_xa.h, 16))
                s_xa.bump(16)
                bld.wait("vector", s_xa, s_xa.n)
                for k in range(4):
                    bld.put("vector", lambda e, t=t, k=k: e.bn_stats(
                        ffn.lnb[:, t % 2, 6 * k:6 * (k + 1)],
                        ffn.xa[:, t % 2, 512 * k:512 * (k + 1)]))
                bld.put("vector", lambda e: e.drain())
                bld.put("vector", lambda e, t=t: e.bn_aggr(
                    ffn.lnb[:, t % 2, 24:26], ffn.lnb[:, t % 2, 0:24]))
                bld.put("vector", lambda e: e.drain())
                bld.put("vector", lambda e, t=t: e.tensor_scalar(
                    ffn.lnb[:, t % 2, 26:27], ffn.lnb[:, t % 2, 25:26], EPS, None, op0=OP.add
                ))
                bld.put("vector", lambda e: e.drain().then_inc(s_lnv.h, 1))
                s_lnv.bump(1)
                bld.wait("scalar", s_lnv, s_lnv.n)
                bld.put("scalar", lambda e, t=t: e.activation(
                    ffn.lnb[:, t % 2, 27:28], ffn.lnb[:, t % 2, 26:27], AF.Sqrt))
                bld.put("scalar", lambda e: e.drain().then_inc(s_sq.h, 1))
                s_sq.bump(1)
                bld.wait("vector", s_sq, s_sq.n)
                bld.put("vector", lambda e, t=t: e.reciprocal(
                    ffn.lnb[:, t % 2, 28:29], ffn.lnb[:, t % 2, 27:28]))
                bld.put("vector", lambda e: e.drain())
                if t >= 2:
                    bld.wait("vector", s_rw, z_ruse[t - 2] + 1)
                bld.put("vector", lambda e, t=t: e.tensor_scalar(
                    ffn.zst[:, t % 2, :], ffn.xa[:, t % 2, :],
                    ffn.lnb[:, t % 2, 24:25], ffn.lnb[:, t % 2, 28:29],
                    op0=OP.subtract, op1=OP.mult))
                bld.put("vector", lambda e: e.drain().then_inc(s_zap.h, 1))
                s_zap.bump(1)

            def emit_z_tr(sbt, ti):
                t = sbt * 4 + ti
                u = ring_use[0]
                z_ruse.append(u)
                bld.wait("tensor", s_zap, 4 * sbt + ti + zap_base + 1)
                bld.wait("tensor", s_rv, u)
                for c in range(HC):
                    def tr(e, t=t, c=c, u=u):
                        e.transpose(ffn.ring[:, 0, c, :],
                                    ffn.zst[:, t % 2, c * 128:(c + 1) * 128], idb_sb[:])
                    bld.put("tensor", tr)
                bld.put("tensor", lambda e: e.drain().then_inc(s_rw.h, 1))
                s_rw.bump(1)
                ring_use[0] += 1
                bld.wait("vector", s_rw, u + 1)
                if sbt >= 2 and ti == 0:
                    bld.wait("vector", s_m1, m1_base + NT1 * (sbt - 1))
                bld.put("vector", lambda e, sbt=sbt, ti=ti, u=u: e.tensor_copy(
                    ffn.zt[:, sbt % 2, :, ti * 128:(ti + 1) * 128], ffn.ring[:, 0, :, :]
                ))
                bld.put("vector", lambda e: e.drain().then_inc(s_rv.h, 1))
                zevs.setdefault(sbt, []).append(s_rv.bump(1))

            zap_base = s_zap.n
            emit_z_ln(0)
            emit_z_ln(1)
            emit_z_tr(0, 0)
            emit_z_ln(2)
            emit_z_tr(0, 1)
            emit_z_ln(3)
            emit_z_tr(0, 2)
            emit_z_tr(0, 3)

            for sb in range(nsb):
                LEAD = 2
                pre = pass1 and gen and sb == 0
                for jj in range(NT1 + (LEAD if pre else 0)):
                    if pre and jj < NT1:
                        emit_w0_slab(jj)
                    j = jj - (LEAD if pre else 0)
                    if j < 0 or j >= NT1:
                        continue
                    gj = NT1 * sb + j
                    if pre:
                        bld.wait("sync", s_w0wr, 16 * (j + 1))
                    if gj >= 3:
                        bld.wait("sync", s_m1, m1_base + gj - 2)
                    bld.put("sync", lambda e, j=j: e.dma_start(
                        ffn.w0b[:, j % 3, :, :], w0t[j].rearrange("c h n -> h c n")
                    ).then_inc(s_w0ld.h, 16))
                    s_w0ld.bump(16)
                    bld.wait("tensor", s_w0ld, s_w0ld.n)
                    if j == 0:
                        bld.wait("tensor", s_rv, zevs[sb][-1])
                    if gj >= 3:
                        bld.wait("tensor", s_gel, gel_base + gj - 2)
                    for c in range(HC):
                        st_f = c == 0
                        last = c == HC - 1

                        def mm(e, sb=sb, j=j, c=c, st_f=st_f, last=last):
                            e.matmul(ffn.dps[:, j % 3, :], ffn.w0b[:, j % 3, c, :],
                                     ffn.zt[:, sb % 2, c, :], start=st_f, stop=last)
                        bld.put("tensor", mm)
                    bld.put("tensor", lambda e: e.drain().then_inc(s_m1.h, 1))
                    s_m1.bump(1)
                    bld.wait("scalar", s_m1, m1_base + gj + 1)
                    if sb > 0 and j == 0:
                        bld.wait("scalar", s_m2q, m2q_base + 4 * HC * sb)
                    bld.put("scalar", lambda e, j=j: e.activation(
                        ffn.dt[:, j, :], ffn.dps[:, j % 3, :], AF.Gelu,
                        bias=b0sb[:, j:j + 1], scale=1.0))
                    bld.put("scalar", lambda e: e.drain().then_inc(s_gel.h, 1))
                    s_gel.bump(1)

                do_w1pre = pre
                for hh in range(HC + (1 if do_w1pre else 0)):
                    if do_w1pre and hh < HC:
                        emit_w1_quarters(hh)
                    h = hh - (1 if do_w1pre else 0)
                    if h < 0 or h >= HC:
                        continue
                    gh = HC * sb + h
                    w1ld_vals = [0] * 4
                    for q in range(4):
                        if do_w1pre:
                            bld.wait("sync", s_w1wr, 16 * (4 * h + q + 1))
                        gq = 4 * gh + q
                        if gq >= 3:
                            bld.wait("sync", s_m2q, m2q_base + gq - 2)
                        bld.put("sync", lambda e, h=h, q=q, sl=gq % 3: e.dma_start(
                            ffn.w1b[:, sl, :, :],
                            w1t[h, q * 16:(q + 1) * 16].rearrange("c n h2 -> n c h2")
                        ).then_inc(s_w1ld.h, 16))
                        w1ld_vals[q] = s_w1ld.bump(16)
                    if h == 0:
                        bld.wait("tensor", s_gel, gel_base + NT1 * (sb + 1))
                    if gh >= 3:
                        bld.wait("tensor", s_ep, ep_base + gh - 2)
                    for c in range(NT1):
                        st_f = c == 0
                        last = c == NT1 - 1
                        lastq = (c % 16) == 15
                        if c % 16 == 0:
                            bld.wait("tensor", s_w1ld, w1ld_vals[c // 16])

                        def mm(e, h=h, c=c, st_f=st_f, last=last, lastq=lastq, gh=gh):
                            sl = (4 * gh + c // 16) % 3
                            i = e.matmul(ffn.etps[:, h % 3, :], ffn.w1b[:, sl, c % 16, :],
                                         ffn.dt[:, c, :], start=st_f, stop=last)
                            if lastq and not last:
                                i.then_inc(s_m2q.h, 1)
                        bld.put("tensor", mm)
                    bld.put("tensor", lambda e: e.drain().then_inc(s_m2q.h, 1))
                    s_m2q.bump(4)
                    bld.wait("vector", s_m2q, m2q_base + 4 * (gh + 1))
                    if sb > 0 and h == 0:
                        bld.wait("vector", s_rw, t_ruse[-1] + 1)
                    bld.put("vector", lambda e, h=h: e.tensor_scalar(
                        ffn.et[:, h, :], ffn.etps[:, h % 3, :], 0.5, hb1sb[:, h:h + 1],
                        op0=OP.mult, op1=OP.add))
                    bld.put("vector", lambda e: e.drain().then_inc(s_ep.h, 1))
                    s_ep.bump(1)
                    if sb + 1 < nsb and h % 4 == 3:
                        emit_z_ln(4 * (sb + 1) + h // 4)
                        if h >= 7:
                            emit_z_tr(sb + 1, h // 4 - 1)

                if sb + 1 < nsb:
                    emit_z_tr(sb + 1, 3)

                for ti in range(4):
                    t = sb * 4 + ti
                    row = t * 128
                    if t >= 2:
                        bld.wait("sync", s_rv, tt_rv[t - 2])
                    bld.put("sync", lambda e, row=row, t=t: e.dma_start(
                        ffn.xr[:, t % 2, :], src[row:row + 128, :]).then_inc(s_xr.h, 16))
                    s_xr.bump(16)
                    u = ring_use[0]
                    t_ruse.append(u)
                    if ti == 0:
                        bld.wait("tensor", s_ep, ep_base + HC * (sb + 1))
                    bld.wait("tensor", s_rv, u)
                    for c in range(HC):
                        def tr(e, ti=ti, c=c, u=u):
                            e.transpose(ffn.ring[:, 0, c, :],
                                        ffn.et[:, c, ti * 128:(ti + 1) * 128], idb_sb[:])
                        bld.put("tensor", tr)
                    bld.put("tensor", lambda e: e.drain().then_inc(s_rw.h, 1))
                    s_rw.bump(1)
                    ring_use[0] += 1
                    bld.wait("vector", s_rw, u + 1)
                    bld.wait("vector", s_xr, s_xr.n)
                    if t >= 2:
                        bld.wait("vector", s_dwr, 16 * (t - 1))
                    bld.put("vector", lambda e, t=t, u=u: e.tensor_tensor(
                        ffn.ost[:, t % 2, :], ffn.xr[:, t % 2, :],
                        ffn.ring[:, 0, :, :].rearrange("p c h -> p (c h)"), op=OP.add
                    ))
                    bld.put("vector", lambda e: e.drain().then_inc(s_rv.h, 1))
                    tt_rv.append(s_rv.bump(1))
                    bld.wait("scalar", s_rv, tt_rv[-1])
                    bld.put("scalar", lambda e, row=row, t=t: e.dma_start(
                        dst[row:row + 128, :], ffn.ost[:, t % 2, :]).then_inc(s_dwr.h, 16))
                    s_dwr.bump(16)
            if pass1:
                pass1_preproc_done[0] = True

        class _NS:
            pass

        def ffn_scope(fctx, sfx, pass1):
            fen = fctx.enter_context
            f = _NS()
            f.zt = fen(nc.sbuf_tensor("zt" + sfx, [128, 2, HC, SB], BF16))
            f.dt = fen(nc.sbuf_tensor("dt" + sfx, [128, NT1, SB], BF16))
            f.et = fen(nc.sbuf_tensor("et" + sfx, [128, HC, SB], BF16))
            f.w0b = fen(nc.sbuf_tensor("w0b" + sfx, [128, 3, HC, 128], BF16))
            f.w1b = fen(nc.sbuf_tensor("w1b" + sfx, [128, 3, 16, 128], BF16))
            io_dt = F32 if pass1 else BF16
            f.xa = fen(nc.sbuf_tensor("xa" + sfx, [128, 2, E], io_dt))
            f.xr = fen(nc.sbuf_tensor("xr" + sfx, [128, 2, E], io_dt))
            f.zst = fen(nc.sbuf_tensor("zst" + sfx, [128, 2, E], BF16))
            f.ost = fen(nc.sbuf_tensor("ost" + sfx, [128, 2, E], BF16 if pass1 else F32))
            f.lnb = fen(nc.sbuf_tensor("lnb" + sfx, [128, 2, 32], F32))
            if pass1:
                f.w0sl = fen(nc.sbuf_tensor("w0sl", [128, 2, E], BF16))
                f.w1sl = fen(nc.sbuf_tensor("w1sl", [128, 4096], BF16))
                f.pst = fen(nc.sbuf_tensor("pstg", [128, 16, 128], BF16))
            f.dps = fen(nc.psum_tensor("dps" + sfx, [128, 3, SB], F32))
            f.etps = fen(nc.psum_tensor("etps" + sfx, [128, 3, SB], F32))
            f.ring = fen(nc.psum_tensor("ring" + sfx, [128, 1, HC, 128], BF16))
            return f

        if "ffn1" in stages:
            with ExitStack() as fctx:
                ffn = ffn_scope(fctx, "1", True)
                gen_ffn(True, ffn)
                bld.barrier(s_stg)
                with nc.Block() as blk:
                    bld.flush(blk)

        def gen_attn(b, at):
            R = b * SEQ
            s_hld = S("hld")
            s_wldq = S("wldq")
            s_ytwr = S("ytwr")
            s_qwr = S("qwr")
            s_vtwr = S("vtwr")
            s_atwr = S("atwr")
            s_otwr = S("otwr")
            s_stl = S("stl")
            s_pmm = S("pmm")
            s_pev = S("pev")
            s_lnv = S("lnv")
            s_sq = S("sq")
            s_zap = S("zap")
            s_sred = S("sred")
            s_exp = S("exp")
            s_asc = S("asc")
            s_gev = S("gev")
            s_h2wr = S("h2wr")
            s_h1wr = sems["h1wr"]
            a_ruse, c_ruse, e_ruse, g_rv = [], [], [], []

            def wload(dstt, w_ap):
                for k in range(4):
                    bld.put("gpsimd", lambda e, w_ap=w_ap, k=k, dstt=dstt: e.dma_start(
                        dstt[:, 4 * k:4 * (k + 1), :],
                        w_ap.rearrange("(c p) f -> p c f", p=128)[:, 4 * k:4 * (k + 1), :]
                    ).then_inc(s_wldq.h, 16))
                    s_wldq.bump(16)
                return s_wldq.n

            half_ctr = [0]
            half_done = {}

            def tile_halved(i, rhs_buf, evac, done_sem):
                for half in range(2):
                    hh = half_ctr[0]
                    half_ctr[0] += 1
                    if hh >= 2:
                        bld.wait("tensor", done_sem, half_done[hh - 2])
                    for cg in (2 * half, 2 * half + 1):
                        for c in range(HC):
                            st_f = c == 0
                            def mm(e, i=i, cg=cg, c=c, st_f=st_f, hh=hh, rhs_buf=rhs_buf):
                                e.matmul(at.pbig[:, hh % 2, (cg % 2) * 512:(cg % 2 + 1) * 512],
                                         at.stt[:, i % 3, c, :],
                                         rhs_buf[:, c, cg * 512:(cg + 1) * 512],
                                         start=st_f, stop=(c == HC - 1))
                            bld.put("tensor", mm)
                    bld.put("tensor", lambda e: e.drain().then_inc(s_pmm.h, 1))
                    s_pmm.bump(1)
                    evac(half, hh, s_pmm.n)

            psum_guard()
            ytwr_base = s_ytwr.n
            bld.wait("sync", s_h1wr, 16 * HC * (b + 1))
            wq_done = wload(at.W_a, wq)
            for i in range(HC):
                if i >= 2:
                    bld.wait("sync", s_zap, s_zap.n - 1)
                bld.put("sync", lambda e, i=i: e.dma_start(
                    at.hst[:, i % 2, :], h1[R + i * 128:R + (i + 1) * 128, :]).then_inc(s_hld.h, 16))
                s_hld.bump(16)
                bld.wait("vector", s_hld, s_hld.n)
                for k in range(4):
                    bld.put("vector", lambda e, i=i, k=k: e.bn_stats(
                        at.lnb[:, i % 2, 6 * k:6 * (k + 1)], at.hst[:, i % 2, 512 * k:512 * (k + 1)]))
                bld.put("vector", lambda e: e.drain())
                bld.put("vector", lambda e, i=i: e.bn_aggr(at.lnb[:, i % 2, 24:26], at.lnb[:, i % 2, 0:24]))
                bld.put("vector", lambda e: e.drain())
                bld.put("vector", lambda e, i=i: e.tensor_scalar(
                    at.lnb[:, i % 2, 26:27], at.lnb[:, i % 2, 25:26], EPS, None, op0=OP.add
                ))
                bld.put("vector", lambda e: e.drain().then_inc(s_lnv.h, 1))
                s_lnv.bump(1)
                bld.wait("scalar", s_lnv, s_lnv.n)
                bld.put("scalar", lambda e, i=i: e.activation(
                    at.lnb[:, i % 2, 27:28], at.lnb[:, i % 2, 26:27], AF.Sqrt))
                bld.put("scalar", lambda e: e.drain().then_inc(s_sq.h, 1))
                s_sq.bump(1)
                bld.wait("vector", s_sq, s_sq.n)
                bld.put("vector", lambda e, i=i: e.reciprocal(at.lnb[:, i % 2, 28:29], at.lnb[:, i % 2, 27:28]))
                bld.put("vector", lambda e: e.drain())
                if i >= 2:
                    bld.wait("vector", s_rw, a_ruse[i - 2] + 1)
                bld.put("vector", lambda e, i=i: e.tensor_scalar(
                    at.ast[:, i % 2, :], at.hst[:, i % 2, :],
                    at.lnb[:, i % 2, 24:25], at.lnb[:, i % 2, 28:29],
                    op0=OP.subtract, op1=OP.mult))
                bld.put("vector", lambda e: e.drain().then_inc(s_zap.h, 1))
                s_zap.bump(1)
                u = ring_use[0]
                a_ruse.append(u)
                bld.wait("tensor", s_zap, s_zap.n)
                bld.wait("tensor", s_rv, u - 1)
                for c in range(HC):
                    last = c == HC - 1

                    def tr(e, i=i, c=c, u=u, last=last):
                        ins = e.transpose(at.ring2[:, u % 2, c, :],
                                          at.ast[:, i % 2, c * 128:(c + 1) * 128], idb_sb[:])
                    bld.put("tensor", tr)
                bld.put("tensor", lambda e: e.drain().then_inc(s_rw.h, 1))
                s_rw.bump(1)
                ring_use[0] += 1
                bld.wait("vector", s_rw, u + 1)
                if i >= 2:
                    bld.wait("vector", s_ytwr, s_ytwr.n - 16)
                bld.put("vector", lambda e, i=i, u=u: e.tensor_copy(
                    at.evst[:, i % 2, :], at.ring2[:, u % 2, :, :].rearrange("p c s -> p (c s)")
                ))
                bld.put("vector", lambda e: e.drain().then_inc(s_rv.h, 1))
                rv_val = s_rv.bump(1)
                bld.wait("scalar", s_rv, rv_val)
                bld.put("scalar", lambda e, i=i: e.dma_start(
                    ytd[i], at.evst[:, i % 2, :]).then_inc(s_ytwr.h, 16))
                s_ytwr.bump(16)
            zap_end = s_zap.n

            def proj_phase(w_buf, out_kind, wld_target):
                psum_guard()
                half_ctr[0] = 0
                half_done.clear()
                pmm_tile = {}
                pmm_start = s_pmm.n
                evst_prev = s_ytwr.n if out_kind == "q" else 0
                for i in range(HC):
                    if i >= 3:
                        bld.wait("sync", s_pmm, s_pmm.n - 4)
                    elif pmm_start:
                        bld.wait("sync", s_pmm, pmm_start)
                    bld.wait("sync", s_ytwr, ytwr_base + 16 * (i + 1))
                    bld.put("sync", lambda e, i=i: e.dma_start(
                        at.stt[:, i % 3, :, :], ytd[i].rearrange("p (c s) -> p c s", c=HC)
                    ).then_inc(s_stl.h, 16))
                    s_stl.bump(16)
                    bld.wait("tensor", s_stl, s_stl.n)
                    if i == 0:
                        bld.wait("tensor", s_wldq, wld_target)

                    def evac(half, hh, pmm_val, i=i):
                        bld.wait("scalar", s_pmm, pmm_val)
                        if out_kind == "q" and half == 0 and i >= 2:
                            bld.wait("scalar", s_qwr, s_qwr.n - 16)
                        if out_kind == "q" and half == 0 and i < 2:
                            bld.wait("scalar", s_ytwr, evst_prev)
                        dest = at.evst[:, i % 2, 1024 * half:1024 * (half + 1)] if out_kind == "q" \
                            else at.W_b[:, i, 1024 * half:1024 * (half + 1)]
                        bld.put("scalar", lambda e, dest=dest, hh=hh: e.copy(
                            dest, at.pbig[:, hh % 2, :]))
                        bld.put("scalar", lambda e: e.drain().then_inc(s_pev.h, 1))
                        half_done[hh] = s_pev.bump(1)
                        if out_kind == "q" and half == 1:
                            bld.wait("scalar", s_pev, half_done[hh])
                            bld.put("scalar", lambda e, i=i: e.dma_start(
                                qd[i * 128:(i + 1) * 128, :], at.evst[:, i % 2, :]
                            ).then_inc(s_qwr.h, 16))
                            s_qwr.bump(16)

                    tile_halved(i, w_buf, evac, s_pev)
                    pmm_tile[i] = s_pmm.n

            wv_done = wload(at.W_b, wv)
            proj_phase(at.W_a, "q", wq_done)

            bld.wait("gpsimd", s_pmm, s_pmm.n)
            wk_done = wload(at.W_a, wk)
            psum_guard()
            half_ctr[0] = 0
            half_done.clear()
            pmm_start = s_pmm.n
            qwr_prev = s_qwr.n
            for i in range(HC):
                if i >= 3:
                    bld.wait("sync", s_pmm, s_pmm.n - 4)
                elif pmm_start:
                    bld.wait("sync", s_pmm, pmm_start)
                bld.put("sync", lambda e, i=i: e.dma_start(
                    at.stt[:, i % 3, :, :], ytd[i].rearrange("p (c s) -> p c s", c=HC)
                ).then_inc(s_stl.h, 16))
                s_stl.bump(16)
                bld.wait("tensor", s_stl, s_stl.n)
                if i == 0:
                    bld.wait("tensor", s_wldq, wv_done)

                def evac(half, hh, pmm_val, i=i):
                    bld.wait("scalar", s_pmm, pmm_val)
                    if half == 0 and i >= 2:
                        bld.wait("scalar", s_rw, c_ruse[i - 2] + 1)
                    if half == 0 and i < 2:
                        bld.wait("scalar", s_qwr, qwr_prev)
                    bld.put("scalar", lambda e, i=i, hh=hh, half=half: e.copy(
                        at.evst[:, i % 2, 1024 * half:1024 * (half + 1)], at.pbig[:, hh % 2, :]))
                    bld.put("scalar", lambda e: e.drain().then_inc(s_pev.h, 1))
                    half_done[hh] = s_pev.bump(1)

                tile_halved(i, at.W_b, evac, s_pev)
                u = ring_use[0]
                c_ruse.append(u)
                bld.wait("tensor", s_pev, half_done[half_ctr[0] - 1])
                bld.wait("tensor", s_rv, u - 1)
                for c in range(HC):
                    def tr(e, i=i, c=c, u=u):
                        e.transpose(at.ring2[:, u % 2, c, :],
                                    at.evst[:, i % 2, c * 128:(c + 1) * 128], idb_sb[:])
                    bld.put("tensor", tr)
                bld.put("tensor", lambda e: e.drain().then_inc(s_rw.h, 1))
                s_rw.bump(1)
                ring_use[0] += 1
                bld.wait("vector", s_rw, u + 1)
                if i >= 2:
                    bld.wait("vector", s_vtwr, s_vtwr.n - 16)
                bld.put("vector", lambda e, i=i, u=u: e.tensor_copy(
                    at.ast[:, i % 2, :], at.ring2[:, u % 2, :, :].rearrange("p c s -> p (c s)")
                ))
                bld.put("vector", lambda e: e.drain().then_inc(s_rv.h, 1))
                rv_val = s_rv.bump(1)
                bld.wait("scalar", s_rv, rv_val)
                bld.put("scalar", lambda e, i=i: e.dma_start(
                    vtd.rearrange("(c p) s -> p c s", p=128)[:, :, i * 128:(i + 1) * 128],
                    at.ast[:, i % 2, :].rearrange("p (c s) -> p c s", c=HC)).then_inc(s_vtwr.h, 16))
                s_vtwr.bump(16)

            proj_phase(at.W_a, "k", wk_done)

            bld.wait("gpsimd", s_pmm, s_pmm.n)
            for k in range(4):
                bld.wait("gpsimd", s_vtwr, s_vtwr.n)
                bld.put("gpsimd", lambda e, k=k: e.dma_start(
                    at.W_a[:, 4 * k:4 * (k + 1), :],
                    vtd.rearrange("(c p) s -> p c s", p=128)[:, 4 * k:4 * (k + 1), :]
                ).then_inc(s_wldq.h, 16))
                s_wldq.bump(16)
            vt_done = s_wldq.n
            psum_guard()
            pmm_start = s_pmm.n
            vtwr_prev = s_vtwr.n
            for i in range(HC):
                if i >= 3:
                    bld.wait("sync", s_pmm, s_pmm.n - 2)
                elif pmm_start:
                    bld.wait("sync", s_pmm, pmm_start)
                if i == 0:
                    bld.wait("sync", s_qwr, s_qwr.n)
                bld.put("sync", lambda e, i=i: e.dma_start(
                    at.stt[:, i % 3, :, :],
                    qd.rearrange("(c p) e -> p c e", p=128)[:, :, i * 128:(i + 1) * 128]
                ).then_inc(s_stl.h, 16))
                s_stl.bump(16)
                bld.wait("tensor", s_stl, s_stl.n)
                if i >= 1:
                    bld.wait("tensor", s_exp, s_exp.n)
                for cg in range(4):
                    for c in range(HC):
                        st_f = c == 0
                        last = (c == HC - 1) and (cg == 3)

                        def mm(e, i=i, cg=cg, c=c, st_f=st_f, last=last):
                            ins = e.matmul(at.pbig[:, cg // 2, (cg % 2) * 512:(cg % 2 + 1) * 512],
                                           at.stt[:, i % 3, c, :],
                                           at.W_b[:, c, cg * 512:(cg + 1) * 512],
                                           start=st_f, stop=(c == HC - 1))
                        bld.put("tensor", mm)
                bld.put("tensor", lambda e: e.drain().then_inc(s_pmm.h, 1))
                s_pmm.bump(1)
                bld.wait("vector", s_pmm, s_pmm.n)
                bld.put("vector", lambda e, i=i: e.reduce_max(
                    at.lnb[:, i % 2, 29:30], at.pbig[:, :, :].rearrange("p a b -> p (a b)"), axis=AX))
                bld.put("vector", lambda e: e.drain())
                bld.put("vector", lambda e, i=i: e.tensor_scalar(
                    at.lnb[:, i % 2, 30:31], at.lnb[:, i % 2, 29:30], -0.125, None, op0=OP.mult
                ))
                bld.put("vector", lambda e: e.drain().then_inc(s_sred.h, 1))
                s_sred.bump(1)
                bld.wait("scalar", s_sred, s_sred.n)
                if i >= 2:
                    bld.wait("scalar", s_asc, s_asc.n - 1)
                bld.put("scalar", lambda e, i=i: e.activation(
                    at.pexp[:, i % 2, :], at.pbig[:, :, :].rearrange("p a b -> p (a b)"),
                    AF.Exp, bias=at.lnb[:, i % 2, 30:31],
                    scale=0.125, accum_out=at.lnb[:, i % 2, 31:32]))
                bld.put("scalar", lambda e: e.drain().then_inc(s_exp.h, 1))
                s_exp.bump(1)
                bld.wait("vector", s_exp, s_exp.n)
                bld.put("vector", lambda e, i=i: e.reciprocal(at.lnb[:, i % 2, 28:29], at.lnb[:, i % 2, 31:32]))
                bld.put("vector", lambda e: e.drain())
                if i >= 2:
                    bld.wait("vector", s_rw, e_ruse[i - 2] + 1)
                else:
                    bld.wait("vector", s_vtwr, vtwr_prev)
                bld.put("vector", lambda e, i=i: e.tensor_scalar(
                    at.ast[:, i % 2, :], at.pexp[:, i % 2, :], at.lnb[:, i % 2, 28:29], None, op0=OP.mult
                ))
                bld.put("vector", lambda e: e.drain().then_inc(s_asc.h, 1))
                s_asc.bump(1)
                u = ring_use[0]
                e_ruse.append(u)
                bld.wait("tensor", s_asc, s_asc.n)
                bld.wait("tensor", s_rv, u - 1)
                for c in range(HC):
                    last = c == HC - 1

                    def tr(e, i=i, c=c, u=u, last=last):
                        ins = e.transpose(at.ring2[:, u % 2, c, :],
                                          at.ast[:, i % 2, c * 128:(c + 1) * 128], idb_sb[:])
                    bld.put("tensor", tr)
                bld.put("tensor", lambda e: e.drain().then_inc(s_rw.h, 1))
                s_rw.bump(1)
                ring_use[0] += 1
                bld.wait("vector", s_rw, u + 1)
                if i >= 2:
                    bld.wait("vector", s_atwr, s_atwr.n - 16)
                bld.put("vector", lambda e, i=i, u=u: e.tensor_copy(
                    at.evst[:, i % 2, :], at.ring2[:, u % 2, :, :].rearrange("p c s -> p (c s)")
                ))
                bld.put("vector", lambda e: e.drain().then_inc(s_rv.h, 1))
                rv_val = s_rv.bump(1)
                bld.wait("scalar", s_rv, rv_val)
                bld.put("scalar", lambda e, i=i: e.dma_start(
                    atd[i], at.evst[:, i % 2, :]).then_inc(s_atwr.h, 16))
                s_atwr.bump(16)

            bld.wait("gpsimd", s_pmm, s_pmm.n)
            wo_done = wload(at.W_b, wo)
            psum_guard()
            half_ctr[0] = 0
            half_done.clear()
            pmm_start = s_pmm.n
            atwr_prev = s_atwr.n
            atwr_base = s_atwr.n - 16 * HC
            for i in range(HC):
                if i >= 3:
                    bld.wait("sync", s_pmm, s_pmm.n - 4)
                elif pmm_start:
                    bld.wait("sync", s_pmm, pmm_start)
                bld.wait("sync", s_atwr, atwr_base + 16 * (i + 1))
                bld.put("sync", lambda e, i=i: e.dma_start(
                    at.stt[:, i % 3, :, :], atd[i].rearrange("p (c s) -> p c s", c=HC)
                ).then_inc(s_stl.h, 16))
                s_stl.bump(16)
                bld.wait("tensor", s_stl, s_stl.n)
                if i == 0:
                    bld.wait("tensor", s_wldq, vt_done)

                def evac(half, hh, pmm_val, i=i):
                    bld.wait("scalar", s_pmm, pmm_val)
                    if half == 0 and i >= 2:
                        bld.wait("scalar", s_otwr, s_otwr.n - 16)
                    if half == 0 and i < 2:
                        bld.wait("scalar", s_atwr, atwr_prev)
                    bld.put("scalar", lambda e, i=i, hh=hh, half=half: e.copy(
                        at.evst[:, i % 2, 1024 * half:1024 * (half + 1)], at.pbig[:, hh % 2, :]))
                    bld.put("scalar", lambda e: e.drain().then_inc(s_pev.h, 1))
                    half_done[hh] = s_pev.bump(1)
                    if half == 1:
                        bld.put("scalar", lambda e, i=i: e.dma_start(
                            otd[i * 128:(i + 1) * 128, :], at.evst[:, i % 2, :]).then_inc(s_otwr.h, 16))
                        s_otwr.bump(16)

                tile_halved(i, at.W_a, evac, s_pev)

            psum_guard()
            half_ctr[0] = 0
            half_done.clear()
            pmm_start = s_pmm.n
            for i in range(HC):
                if i >= 3:
                    bld.wait("sync", s_pmm, s_pmm.n - 4)
                elif pmm_start:
                    bld.wait("sync", s_pmm, pmm_start)
                if i >= 2:
                    bld.wait("sync", s_gev, g_rv[2 * i - 3])
                if i == 0:
                    bld.wait("sync", s_otwr, s_otwr.n)
                bld.put("sync", lambda e, i=i: e.dma_start(
                    at.stt[:, i % 3, :, :],
                    otd.rearrange("(c p) s -> p c s", p=128)[:, :, i * 128:(i + 1) * 128]
                ).then_inc(s_stl.h, 16))
                s_stl.bump(16)
                if i < 2:
                    bld.wait("sync", s_zap, zap_end)
                bld.put("sync", lambda e, i=i: e.dma_start(
                    at.hst[:, i % 2, :], h1[R + i * 128:R + (i + 1) * 128, :]).then_inc(s_hld.h, 16))
                s_hld.bump(16)
                bld.wait("tensor", s_stl, s_stl.n)
                if i == 0:
                    bld.wait("tensor", s_wldq, wo_done)
                hld_val = s_hld.n

                def evac(half, hh, pmm_val, i=i, hld_val=hld_val):
                    bld.wait("vector", s_pmm, pmm_val)
                    bld.wait("vector", s_hld, hld_val)
                    if half == 0 and i >= 2:
                        bld.wait("vector", s_h2wr, s_h2wr.n - 16)
                    bld.put("vector", lambda e, i=i, hh=hh, half=half: e.tensor_tensor(
                        at.h2st[:, i % 2, 1024 * half:1024 * (half + 1)],
                        at.hst[:, i % 2, 1024 * half:1024 * (half + 1)],
                        at.pbig[:, hh % 2, :], op=OP.add))
                    bld.put("vector", lambda e: e.drain().then_inc(s_gev.h, 1))
                    half_done[hh] = s_gev.bump(1)
                    g_rv.append(half_done[hh])
                    if half == 1:
                        bld.wait("scalar", s_gev, half_done[hh])
                        bld.put("scalar", lambda e, i=i: e.dma_start(
                            h2[R + i * 128:R + (i + 1) * 128, :], at.h2st[:, i % 2, :]
                        ).then_inc(s_h2wr.h, 16))
                        s_h2wr.bump(16)

                tile_halved(i, at.W_b, evac, s_gev)

        for b, stg in ((0, "attn0"), (1, "attn1")):
            if stg not in stages:
                continue
            with ExitStack() as actx:
                aen = actx.enter_context
                sfx = "a%d" % b
                at = _NS()
                at.W_a = aen(nc.sbuf_tensor("W_a" + sfx, [128, HC, E], BF16))
                at.W_b = aen(nc.sbuf_tensor("W_b" + sfx, [128, HC, E], BF16))
                at.stt = aen(nc.sbuf_tensor("stt" + sfx, [128, 3, HC, 128], BF16))
                at.evst = aen(nc.sbuf_tensor("evst" + sfx, [128, 2, E], BF16))
                at.pexp = aen(nc.sbuf_tensor("pexp" + sfx, [128, 2, E], BF16))
                at.ast = aen(nc.sbuf_tensor("ast" + sfx, [128, 2, E], BF16))
                at.hst = aen(nc.sbuf_tensor("hst" + sfx, [128, 2, E], BF16))
                at.h2st = aen(nc.sbuf_tensor("h2st" + sfx, [128, 2, E], BF16))
                at.lnb = aen(nc.sbuf_tensor("lnb" + sfx, [128, 2, 32], F32))
                at.pbig = aen(nc.psum_tensor("pbig" + sfx, [128, 2, 1024], F32))
                at.ring2 = aen(nc.psum_tensor("ring2" + sfx, [128, 2, HC, 128], BF16))
                gen_attn(b, at)
                bld.barrier(s_stg)
                with nc.Block() as blk:
                    bld.flush(blk)

        if "ffn3" in stages:
            with ExitStack() as fctx:
                ffn = ffn_scope(fctx, "3", False)
                gen_ffn(False, ffn)
                bld.barrier(s_stg)
                with nc.Block() as blk:
                    bld.flush(blk)

        final_waits = []
        if "ffn3" in stages:
            final_waits.append(("ywr", sems["ywr"].n))
        for nm in debug_out:
            wrs = {"h1": "h1wr", "h2": "h2wr"}.get(nm)
            if wrs and wrs in sems:
                final_waits.append((wrs, sems[wrs].n))
        with nc.Block() as blk:
            @blk.sync
            def _(sy):
                for nm, v in final_waits:
                    sy.wait_ge(sems[nm].h, v)

    return nc


def _prep_inputs(L0_attn_x, L0_attn_Wq, L0_attn_Wk, L0_attn_Wv, L0_attn_Wo,
                 L0_ffn_W0, L0_ffn_bias0, L0_ffn_W1, L0_ffn_bias1):
    f = lambda a: np.ascontiguousarray(np.asarray(a, dtype=np.float32))
    xs = f(L0_attn_x)
    common = dict(
        wq=f(L0_attn_Wq), wk=f(L0_attn_Wk), wv=f(L0_attn_Wv), wo=f(L0_attn_Wo),
        w0=f(L0_ffn_W0), w1=f(L0_ffn_W1),
        b0r=f(L0_ffn_bias0).reshape(NT1, 128), b1r=f(L0_ffn_bias1).reshape(HC, 128),
        idf=np.eye(128, dtype=np.float32), idb=np.eye(128, dtype=ml_dtypes.bfloat16),
    )
    maps = []
    for c in range(NCORES):
        m = dict(common)
        m["x"] = np.ascontiguousarray(xs[2 * c:2 * c + 2].reshape(TPC, E))
        maps.append(m)
    return maps


def kernel(L0_attn_x, L0_attn_Wq, L0_attn_Wk, L0_attn_Wv, L0_attn_Wo,
           L0_ffn_W0, L0_ffn_bias0, L0_ffn_W1, L0_ffn_bias1):
    maps = _prep_inputs(L0_attn_x, L0_attn_Wq, L0_attn_Wk, L0_attn_Wv, L0_attn_Wo,
                        L0_ffn_W0, L0_ffn_bias0, L0_ffn_W1, L0_ffn_bias1)
    nc = _build()
    res = run_bass_kernel_spmd(nc, maps, core_ids=list(range(NCORES)))
    out = np.concatenate(
        [np.asarray(r["y"], dtype=np.float32).reshape(2, SEQ, E) for r in res.results], axis=0)
    return out
